# version 01
# baseline (speedup 1.0000x reference)
import contextlib
import numpy as np
import ml_dtypes
import concourse.bass as bass
import concourse.mybir as mybir
from concourse.bass_utils import run_bass_kernel_spmd

F32 = mybir.dt.float32
BF16 = mybir.dt.bfloat16
AF = mybir.ActivationFunctionType
ALU = mybir.AluOpType
AX = mybir.AxisListType.X
BF = ml_dtypes.bfloat16

NCORES = 8
B, TF, C = 4, 4096, 1024
TC = 2048
TT = 512
NT = TC // TT
TH = 2560
NTH = TH // TT
ENGS = ("sync", "tensor", "vector", "scalar", "gpsimd")
HALF_PI = float(np.pi / 2)
CSC_SHIFT = float(0.5 * np.log(512.0))
NEG_BIG = -3.0e38


class PB:

    def __init__(self, nc, es):
        self.nc = nc
        self.block = es.enter_context(nc.Block())
        self.sems = {e: es.enter_context(nc.semaphore("pb_" + e)) for e in ENGS}
        self.cnt = {e: 0 for e in ENGS}

    def phase(self, **fns):
        pre = dict(self.cnt)
        for ename, fn in fns.items():
            sem = self.sems[ename]
            added = [0]

            def body(eng, fn=fn, ename=ename, sem=sem, added=added, pre=pre):
                for o in ENGS:
                    if o != ename and pre[o] > 0:
                        eng.wait_ge(self.sems[o], pre[o])

                def inc(inst, n=1, wait=False):
                    inst.then_inc(sem, n)
                    added[0] += n
                    if wait:
                        eng.wait_ge(sem, pre[ename] + added[0])

                fn(eng, inc)

            getattr(self.block, ename)(body)
            assert added[0] > 0, f"phase body on {ename} incremented nothing"
            self.cnt[ename] += added[0]

    def finish(self):
        pre = dict(self.cnt)

        def body(eng):
            for o in ENGS:
                if pre[o] > 0:
                    eng.wait_ge(self.sems[o], pre[o])

        self.block.sync(body)


def ch(t, c, W):
    return t[:, c * W:(c + 1) * W]


def load_fm(eng, inc, sbuf, dram, nch, W, t0):
    for c in range(nch):
        inc(eng.dma_start(out=ch(sbuf, c, W), in_=dram[c * 128:(c + 1) * 128, t0:t0 + W]), 16)


def store_fm(eng, inc, dram, sbuf, nch, W, t0):
    for c in range(nch):
        inc(eng.dma_start(out=dram[c * 128:(c + 1) * 128, t0:t0 + W], in_=ch(sbuf, c, W)), 16)


class LNKit:

    def __init__(self, nc, es, tag, nch, W):
        self.nch, self.W = nch, W
        self.xbf = es.enter_context(nc.sbuf_tensor(f"{tag}_xbf", [128, nch * W], BF16))
        self.d = es.enter_context(nc.sbuf_tensor(f"{tag}_d", [128, nch * W], F32))
        self.rstd_a = es.enter_context(nc.sbuf_tensor(f"{tag}_ra", [128, W], F32))
        self.rstd = es.enter_context(nc.sbuf_tensor(f"{tag}_rb", [128, W], F32))


def emit_ln(pp, kit, consts, src, pa, pb, out_bf=None, out_f32=None, eps="eps5"):
    nch, W = kit.nch, kit.W
    Cn = nch * 128
    ones_bf = consts["ones_bf"]

    def cast(v, inc):
        for c in range(nch):
            ins = v.tensor_copy(out=ch(kit.xbf, c, W), in_=ch(src, c, W))
        inc(ins)
    pp.phase(vector=cast)

    def mm_mu(t, inc):
        for c in range(nch):
            ins = t.matmul(pa[:, 0:W], ones_bf, ch(kit.xbf, c, W),
                           start=(c == 0), stop=(c == nch - 1))
        inc(ins)
    pp.phase(tensor=mm_mu)

    def sub(v, inc):
        for c in range(nch):
            ins = v.scalar_tensor_tensor(out=ch(kit.d, c, W), in0=pa[:, 0:W],
                                         scalar=-1.0 / Cn, in1=ch(src, c, W),
                                         op0=ALU.mult, op1=ALU.add)
        inc(ins)
    pp.phase(vector=sub)

    def sq(s, inc):
        for c in range(nch):
            ins = s.activation(out=ch(kit.xbf, c, W), in_=ch(kit.d, c, W), func=AF.Square)
        inc(ins)
    pp.phase(scalar=sq)

    def mm_var(t, inc):
        for c in range(nch):
            ins = t.matmul(pb[:, 0:W], ones_bf, ch(kit.xbf, c, W),
                           start=(c == 0), stop=(c == nch - 1))
        inc(ins)
    pp.phase(tensor=mm_var)

    def rstd(s, inc):
        s.activation(out=kit.rstd_a[:], in_=pb[:, 0:W], func=AF.Ln, scale=1.0 / Cn,
                     bias=consts[eps])
        ins = s.activation(out=kit.rstd[:], in_=kit.rstd_a[:], func=AF.Exp, scale=-0.5)
        inc(ins)
    pp.phase(scalar=rstd)

    if out_bf is not None or out_f32 is not None:
        def apply(v, inc):
            for c in range(nch):
                if out_bf is not None:
                    ins = v.tensor_tensor(out=ch(out_bf, c, W), in0=ch(kit.d, c, W),
                                          in1=kit.rstd[:], op=ALU.mult)
                if out_f32 is not None:
                    ins = v.tensor_tensor(out=ch(out_f32, c, W), in0=ch(kit.d, c, W),
                                          in1=kit.rstd[:], op=ALU.mult)
            inc(ins)
        pp.phase(vector=apply)


def make_consts(nc, es, pp, specs):
    consts = {}
    vals = {}
    for name, v in specs.items():
        t = es.enter_context(nc.sbuf_tensor("c_" + name, [128, 1], F32))
        consts[name] = t[:]
        vals[name] = v
    ones_bf = es.enter_context(nc.sbuf_tensor("c_onesbf", [128, 128], BF16))
    ones_row = es.enter_context(nc.sbuf_tensor("c_onesrow", [1, 128], F32))
    consts["ones_bf"] = ones_bf[:]
    consts["ones_row"] = ones_row[:]

    def setc(v, inc):
        for name, val in vals.items():
            v.memset(consts[name], float(val))
        v.memset(ones_bf[:], 1.0)
        ins = v.memset(ones_row[:], 1.0)
        inc(ins)
    pp.phase(vector=setc)
    return consts


def cast_weight(pp, dram, rows, cols, stage, out_bf):
    nk = rows // 128

    def load(e, inc):
        for kc in range(nk):
            inc(e.dma_start(out=stage[:, kc * cols:(kc + 1) * cols],
                            in_=dram[kc * 128:(kc + 1) * 128, :]), 16)
    pp.phase(sync=load)

    def cast(v, inc):
        ins = v.tensor_copy(out=out_bf[:, 0:nk * cols], in_=stage[:, 0:nk * cols])
        inc(ins)
    pp.phase(vector=cast)



def build_l1_gates():
    nc = bass.Bass("TRN2", target_bir_lowering=False)
    xs = nc.dram_tensor("xs", [C, TC], F32, kind="ExternalInput").ap()
    wiT = nc.dram_tensor("wiT", [C, C], F32, kind="ExternalInput").ap()
    wrT = nc.dram_tensor("wrT", [C, C], F32, kind="ExternalInput").ap()
    bi = nc.dram_tensor("bi", [128, 8], F32, kind="ExternalInput").ap()
    br = nc.dram_tensor("br", [128, 8], F32, kind="ExternalInput").ap()
    lam = nc.dram_tensor("lam", [1, 1], F32, kind="ExternalInput").ap()
    at_o = nc.dram_tensor("at_o", [C, TC], F32, kind="ExternalOutput").ap()
    gx_o = nc.dram_tensor("gx_o", [C, TC], F32, kind="ExternalOutput").ap()

    with contextlib.ExitStack() as es:
        pp = PB(nc, es)
        stage = es.enter_context(nc.sbuf_tensor("stage", [128, 8 * C], F32))
        wi_bf = es.enter_context(nc.sbuf_tensor("wi_bf", [128, 8 * C], BF16))
        wr_bf = es.enter_context(nc.sbuf_tensor("wr_bf", [128, 8 * C], BF16))
        bi_sb = es.enter_context(nc.sbuf_tensor("bi_sb", [128, 8], F32))
        br_sb = es.enter_context(nc.sbuf_tensor("br_sb", [128, 8], F32))
        lam_sb = es.enter_context(nc.sbuf_tensor("lam_sb", [1, 1], F32))
        sp_t = es.enter_context(nc.sbuf_tensor("sp_t", [1, 1], F32))
        sp_t2 = es.enter_context(nc.sbuf_tensor("sp_t2", [1, 1], F32))
        s8_bc = es.enter_context(nc.sbuf_tensor("s8_bc", [128, 1], F32))
        x_t = es.enter_context(nc.sbuf_tensor("x_t", [128, 8 * TT], F32))
        xin_bf = es.enter_context(nc.sbuf_tensor("xin_bf", [128, 8 * TT], BF16))
        it_f = es.enter_context(nc.sbuf_tensor("it_f", [128, 8 * TT], F32))
        at_f = es.enter_context(nc.sbuf_tensor("at_f", [128, 8 * TT], F32))
        gx_f = es.enter_context(nc.sbuf_tensor("gx_f", [128, 8 * TT], F32))
        t1 = es.enter_context(nc.sbuf_tensor("t1", [128, TT], F32))
        t2 = es.enter_context(nc.sbuf_tensor("t2", [128, TT], F32))
        kit = LNKit(nc, es, "ln", 8, TT)
        pa = es.enter_context(nc.psum_tensor("pa", [128, TT], F32))
        pb = es.enter_context(nc.psum_tensor("pb", [128, TT], F32))
        pd = es.enter_context(nc.psum_tensor("pd", [128, 1], F32))

        consts = make_consts(nc, es, pp, {"eps5": 1e-5, "one": 1.0})

        cast_weight(pp, wiT, C, C, stage, wi_bf)
        cast_weight(pp, wrT, C, C, stage, wr_bf)

        def load_small(e, inc):
            inc(e.dma_start(out=bi_sb[:], in_=bi), 16)
            inc(e.dma_start(out=br_sb[:], in_=br), 16)
            inc(e.dma_start(out=lam_sb[:], in_=lam), 16)
        pp.phase(sync=load_small)

        def sp1(s, inc):
            inc(s.activation(out=sp_t[:], in_=lam_sb[:], func=AF.Abs), wait=True)
            inc(s.activation(out=sp_t[:], in_=sp_t[:], func=AF.Exp, scale=-1.0), wait=True)
            inc(s.activation(out=sp_t[:], in_=sp_t[:], func=AF.Ln,
                             bias=consts["one"][0:1, :]), wait=True)
            inc(s.activation(out=sp_t2[:], in_=lam_sb[:], func=AF.Relu))
        pp.phase(scalar=sp1)

        def sp2(v, inc):
            inc(v.tensor_tensor(out=sp_t2[:], in0=sp_t2[:], in1=sp_t[:], op=ALU.add),
                wait=True)
            inc(v.tensor_scalar(out=sp_t2[:], in0=sp_t2[:], scalar1=-8.0,
                                scalar2=None, op0=ALU.mult))
        pp.phase(vector=sp2)

        def sp3(t, inc):
            ins = t.matmul(pd[:, 0:1], consts["ones_row"], sp_t2[:], start=True, stop=True)
            inc(ins)
        pp.phase(tensor=sp3)

        def sp4(v, inc):
            ins = v.tensor_copy(out=s8_bc[:], in_=pd[:, 0:1])
            inc(ins)
        pp.phase(vector=sp4)

        for tt in range(NT):
            t0 = tt * TT

            def load_x(e, inc, t0=t0):
                load_fm(e, inc, x_t, xs, 8, TT, t0)
            pp.phase(sync=load_x)

            emit_ln(pp, kit, consts, x_t, pa, pb, out_bf=xin_bf)

            for which in ("i", "r"):
                w_bf = wi_bf if which == "i" else wr_bf
                b_sb = bi_sb if which == "i" else br_sb
                for dc in range(8):
                    ps = pa if dc % 2 == 0 else pb

                    def mm(t, inc, dc=dc, ps=ps, w_bf=w_bf):
                        for kc in range(8):
                            ins = t.matmul(
                                ps[:, 0:TT],
                                w_bf[:, kc * C + dc * 128: kc * C + (dc + 1) * 128],
                                ch(xin_bf, kc, TT), start=(kc == 0), stop=(kc == 7))
                        inc(ins)
                    pp.phase(tensor=mm)

                    if which == "i":
                        def ev(s, inc, dc=dc, ps=ps, b_sb=b_sb):
                            ins = s.activation(out=ch(it_f, dc, TT), in_=ps[:, 0:TT],
                                               func=AF.Sigmoid, bias=b_sb[:, dc:dc + 1])
                            inc(ins)
                        pp.phase(scalar=ev)
                    else:
                        def ev(s, inc, dc=dc, ps=ps, b_sb=b_sb):
                            s.activation(out=t1[:], in_=ps[:, 0:TT],
                                         func=AF.Sigmoid, bias=b_sb[:, dc:dc + 1])
                            ins = s.activation(out=ch(at_f, dc, TT), in_=t1[:],
                                               func=AF.Exp, scale=s8_bc[:])
                            inc(ins)
                        pp.phase(scalar=ev)

            for dc in range(8):
                def g1(s, inc, dc=dc):
                    s.activation(out=t1[:], in_=ch(at_f, dc, TT), func=AF.Square)
                    s.activation(out=t1[:], in_=t1[:], func=AF.Ln, scale=-1.0,
                                 bias=consts["one"])
                    ins = s.activation(out=t2[:], in_=t1[:], func=AF.Exp, scale=0.5)
                    inc(ins)
                pp.phase(scalar=g1)

                def g2(v, inc, dc=dc):
                    v.tensor_tensor(out=ch(gx_f, dc, TT), in0=ch(it_f, dc, TT),
                                    in1=ch(xin_bf, dc, TT), op=ALU.mult)
                    ins = v.tensor_tensor(out=ch(gx_f, dc, TT), in0=ch(gx_f, dc, TT),
                                          in1=t2[:], op=ALU.mult)
                    inc(ins)
                pp.phase(vector=g2)

            def store(e, inc, t0=t0):
                store_fm(e, inc, at_o, at_f, 8, TT, t0)
                store_fm(e, inc, gx_o, gx_f, 8, TT, t0)
            pp.phase(gpsimd=store)

        pp.finish()
    return nc



def build_l2_scan():
    nc = bass.Bass("TRN2", target_bir_lowering=False)
    at_i = nc.dram_tensor("at_i", [C // 2, TF], F32, kind="ExternalInput").ap()
    gx_i = nc.dram_tensor("gx_i", [C // 2, TF], F32, kind="ExternalInput").ap()
    ys_o = nc.dram_tensor("ys_o", [C // 2, TF], F32, kind="ExternalOutput").ap()
    with contextlib.ExitStack() as es:
        pp = PB(nc, es)
        a_t = es.enter_context(nc.sbuf_tensor("a_t", [128, TF], F32))
        g_t = es.enter_context(nc.sbuf_tensor("g_t", [128, TF], F32))
        y_t = es.enter_context(nc.sbuf_tensor("y_t", [128, TF], F32))
        for c in range(4):
            def load(e, inc, c=c):
                inc(e.dma_start(out=a_t[:], in_=at_i[c * 128:(c + 1) * 128, :]), 16)
                inc(e.dma_start(out=g_t[:], in_=gx_i[c * 128:(c + 1) * 128, :]), 16)
            pp.phase(sync=load)

            def scan(v, inc):
                ins = v.tensor_tensor_scan(out=y_t[:], data0=a_t[:], data1=g_t[:],
                                           initial=0.0, op0=ALU.mult, op1=ALU.add)
                inc(ins)
            pp.phase(vector=scan)

            def store(e, inc, c=c):
                inc(e.dma_start(out=ys_o[c * 128:(c + 1) * 128, :], in_=y_t[:]), 16)
            pp.phase(gpsimd=store)
        pp.finish()
    return nc



def build_l3_ffn0():
    nc = bass.Bass("TRN2", target_bir_lowering=False)
    xs = nc.dram_tensor("xs", [C, TC], F32, kind="ExternalInput").ap()
    ys = nc.dram_tensor("ys", [C, TC], F32, kind="ExternalInput").ap()
    th = nc.dram_tensor("th", [C, C], F32, kind="ExternalInput").ap()
    thn = nc.dram_tensor("thn", [C, C], F32, kind="ExternalInput").ap()
    ph = nc.dram_tensor("ph", [C, C], F32, kind="ExternalInput").ap()
    phn = nc.dram_tensor("phn", [C, C], F32, kind="ExternalInput").ap()
    x2_o = nc.dram_tensor("x2_o", [C, TC], F32, kind="ExternalOutput").ap()
    with contextlib.ExitStack() as es:
        pp = PB(nc, es)
        stage_a = es.enter_context(nc.sbuf_tensor("stage_a", [128, 8 * C], F32))
        stage_b = es.enter_context(nc.sbuf_tensor("stage_b", [128, 8 * C], F32))
        cos_bf = es.enter_context(nc.sbuf_tensor("cos_bf", [128, 8 * C], BF16))
        sin_bf = es.enter_context(nc.sbuf_tensor("sin_bf", [128, 8 * C], BF16))
        x1_t = es.enter_context(nc.sbuf_tensor("x1_t", [128, 8 * TT], F32))
        ys_t = es.enter_context(nc.sbuf_tensor("ys_t", [128, 8 * TT], F32))
        fin_bf = es.enter_context(nc.sbuf_tensor("fin_bf", [128, 8 * TT], BF16))
        r_f = es.enter_context(nc.sbuf_tensor("r_f", [128, 8 * TT], F32))
        sr = es.enter_context(nc.sbuf_tensor("sr", [128, TT], F32))
        si = es.enter_context(nc.sbuf_tensor("si", [128, TT], F32))
        kit = LNKit(nc, es, "ln", 8, TT)
        pa = es.enter_context(nc.psum_tensor("pa", [128, TT], F32))
        pb = es.enter_context(nc.psum_tensor("pb", [128, TT], F32))
        consts = make_consts(nc, es, pp, {"eps5": 1e-5, "one": 1.0, "halfpi": HALF_PI})

        for (wa, wb, out_bf, bias) in ((th, thn, cos_bf, "halfpi"),
                                       (ph, phn, sin_bf, None)):
            def load(e, inc, wa=wa, wb=wb):
                for kc in range(8):
                    inc(e.dma_start(out=stage_a[:, kc * C:(kc + 1) * C],
                                    in_=wa[kc * 128:(kc + 1) * 128, :]), 16)
                    inc(e.dma_start(out=stage_b[:, kc * C:(kc + 1) * C],
                                    in_=wb[kc * 128:(kc + 1) * 128, :]), 16)
            pp.phase(sync=load)

            def add(v, inc):
                ins = v.tensor_tensor(out=stage_a[:], in0=stage_a[:], in1=stage_b[:],
                                      op=ALU.add)
                inc(ins)
            pp.phase(vector=add)

            def trig(s, inc, out_bf=out_bf, bias=bias):
                if bias is None:
                    ins = s.activation(out=out_bf[:], in_=stage_a[:], func=AF.Sin)
                else:
                    ins = s.activation(out=out_bf[:], in_=stage_a[:], func=AF.Sin,
                                       bias=consts[bias])
                inc(ins)
            pp.phase(scalar=trig)

        for tt in range(NT):
            t0 = tt * TT

            def load_x(e, inc, t0=t0):
                load_fm(e, inc, x1_t, xs, 8, TT, t0)
                load_fm(e, inc, ys_t, ys, 8, TT, t0)
            pp.phase(sync=load_x)

            emit_ln(pp, kit, consts, ys_t, pa, pb, out_f32=ys_t)

            def addres(v, inc):
                for c in range(8):
                    ins = v.tensor_tensor(out=ch(x1_t, c, TT), in0=ch(x1_t, c, TT),
                                          in1=ch(ys_t, c, TT), op=ALU.add)
                inc(ins)
            pp.phase(vector=addres)

            emit_ln(pp, kit, consts, x1_t, pa, pb, out_bf=fin_bf)

            for dc in range(8):
                def mm(t, inc, dc=dc):
                    for kc in range(8):
                        t.matmul(pa[:, 0:TT],
                                 cos_bf[:, kc * C + dc * 128: kc * C + (dc + 1) * 128],
                                 ch(fin_bf, kc, TT), start=(kc == 0), stop=(kc == 7))
                    for kc in range(8):
                        ins = t.matmul(
                            pb[:, 0:TT],
                            sin_bf[:, kc * C + dc * 128: kc * C + (dc + 1) * 128],
                            ch(fin_bf, kc, TT), start=(kc == 0), stop=(kc == 7))
                    inc(ins)
                pp.phase(tensor=mm)

                def sq(s, inc):
                    s.activation(out=sr[:], in_=pa[:, 0:TT], func=AF.Square)
                    ins = s.activation(out=si[:], in_=pb[:, 0:TT], func=AF.Square)
                    inc(ins)
                pp.phase(scalar=sq)

                def add2(v, inc):
                    ins = v.tensor_tensor(out=sr[:], in0=sr[:], in1=si[:], op=ALU.add)
                    inc(ins)
                pp.phase(vector=add2)

                def root(s, inc, dc=dc):
                    s.activation(out=sr[:], in_=sr[:], func=AF.Ln)
                    ins = s.activation(out=ch(r_f, dc, TT), in_=sr[:], func=AF.Exp,
                                       scale=0.5)
                    inc(ins)
                pp.phase(scalar=root)

            emit_ln(pp, kit, consts, r_f, pa, pb, out_f32=r_f)

            def addres2(v, inc):
                for c in range(8):
                    ins = v.tensor_tensor(out=ch(x1_t, c, TT), in0=ch(x1_t, c, TT),
                                          in1=ch(r_f, c, TT), op=ALU.add)
                inc(ins)
            pp.phase(vector=addres2)

            def store(e, inc, t0=t0):
                store_fm(e, inc, x2_o, x1_t, 8, TT, t0)
            pp.phase(gpsimd=store)
        pp.finish()
    return nc



def build_l4a_up():
    nc = bass.Bass("TRN2", target_bir_lowering=False)
    x2h = nc.dram_tensor("x2h", [C, TH], F32, kind="ExternalInput").ap()
    wup = nc.dram_tensor("wup", [C, 4096], F32, kind="ExternalInput").ap()
    xml_o = nc.dram_tensor("xml_o", [2048, TH], BF16, kind="ExternalOutput").ap()
    z_o = nc.dram_tensor("z_o", [2048, TC], BF16, kind="ExternalOutput").ap()
    with contextlib.ExitStack() as es:
        pp = PB(nc, es)
        stage = es.enter_context(nc.sbuf_tensor("stage", [128, 8 * 2048], F32))
        wh_bf = es.enter_context(nc.sbuf_tensor("wh_bf", [128, 8 * 2048], BF16))
        x_t = es.enter_context(nc.sbuf_tensor("x_t", [128, 8 * TT], F32))
        fin_bf = es.enter_context(nc.sbuf_tensor("fin_bf", [128, 8 * TH], BF16))
        out_bf = es.enter_context(nc.sbuf_tensor("out_bf", [128, TT], BF16))
        kit = LNKit(nc, es, "ln", 8, TT)
        pa = es.enter_context(nc.psum_tensor("pa", [128, TT], F32))
        pb = es.enter_context(nc.psum_tensor("pb", [128, TT], F32))
        consts = make_consts(nc, es, pp, {"eps5": 1e-5, "one": 1.0})

        for tt in range(NTH):
            t0 = tt * TT

            def load_x(e, inc, t0=t0):
                load_fm(e, inc, x_t, x2h, 8, TT, t0)
            pp.phase(sync=load_x)

            emit_ln(pp, kit, consts, x_t, pa, pb)

            def apply(v, inc, t0=t0):
                for c in range(8):
                    ins = v.tensor_tensor(out=fin_bf[:, c * TH + t0: c * TH + t0 + TT],
                                          in0=ch(kit.d, c, TT), in1=kit.rstd[:],
                                          op=ALU.mult)
                inc(ins)
            pp.phase(vector=apply)

        for half, (dram_out, ntt, toff, wcol0) in enumerate(
                ((z_o, NT, 512, 2048), (xml_o, NTH, 0, 0))):
            def loadw(e, inc, wcol0=wcol0):
                for kc in range(8):
                    inc(e.dma_start(out=stage[:, kc * 2048:(kc + 1) * 2048],
                                    in_=wup[kc * 128:(kc + 1) * 128, wcol0:wcol0 + 2048]), 16)
            pp.phase(sync=loadw)

            def castw(v, inc):
                ins = v.tensor_copy(out=wh_bf[:], in_=stage[:])
                inc(ins)
            pp.phase(vector=castw)

            for tt in range(ntt):
                t0 = toff + tt * TT
                for dc in range(16):
                    ps = pa if dc % 2 == 0 else pb

                    def mm(t, inc, dc=dc, ps=ps, t0=t0):
                        for kc in range(8):
                            ins = t.matmul(
                                ps[:, 0:TT],
                                wh_bf[:, kc * 2048 + dc * 128: kc * 2048 + (dc + 1) * 128],
                                fin_bf[:, kc * TH + t0: kc * TH + t0 + TT],
                                start=(kc == 0), stop=(kc == 7))
                        inc(ins)
                    pp.phase(tensor=mm)

                    def ev(v, inc, ps=ps):
                        ins = v.tensor_copy(out=out_bf[:], in_=ps[:, 0:TT])
                        inc(ins)
                    pp.phase(vector=ev)

                    def st(e, inc, dc=dc, t0=t0, dram_out=dram_out, toff=toff):
                        inc(e.dma_start(out=dram_out[dc * 128:(dc + 1) * 128,
                                                     t0 - toff: t0 - toff + TT],
                                        in_=out_bf[:]), 16)
                    pp.phase(gpsimd=st)
        pp.finish()
    return nc



def build_l4b_qkv():
    nc = bass.Bass("TRN2", target_bir_lowering=False)
    xml = nc.dram_tensor("xml", [2048, TH], BF16, kind="ExternalInput").ap()
    bq = nc.dram_tensor("bq", [16 * 128, 128], F32, kind="ExternalInput").ap()
    bk = nc.dram_tensor("bk", [16 * 128, 128], F32, kind="ExternalInput").ap()
    bv = nc.dram_tensor("bv", [16 * 128, 128], F32, kind="ExternalInput").ap()
    bqT = nc.dram_tensor("bqT", [16 * 128, 128], F32, kind="ExternalInput").ap()
    bkT = nc.dram_tensor("bkT", [16 * 128, 128], F32, kind="ExternalInput").ap()
    bvT = nc.dram_tensor("bvT", [16 * 128, 128], F32, kind="ExternalInput").ap()
    wig_q = nc.dram_tensor("wig_q", [2048, 8], F32, kind="ExternalInput").ap()
    wig_k = nc.dram_tensor("wig_k", [2048, 8], F32, kind="ExternalInput").ap()
    wig_v = nc.dram_tensor("wig_v", [2048, 8], F32, kind="ExternalInput").ap()
    bg = nc.dram_tensor("bg", [8, 1], F32, kind="ExternalInput").ap()
    cw = nc.dram_tensor("cw", [128, 64], F32, kind="ExternalInput").ap()
    cb = nc.dram_tensor("cb", [128, 16], F32, kind="ExternalInput").ap()
    q_o = nc.dram_tensor("q_o", [2048, TC], BF16, kind="ExternalOutput").ap()
    k_o = nc.dram_tensor("k_o", [2048, TC], BF16, kind="ExternalOutput").ap()
    v_o = nc.dram_tensor("v_o", [TC, 2048], BF16, kind="ExternalOutput").ap()
    xact_o = nc.dram_tensor("xact_o", [2048, TC], BF16, kind="ExternalOutput").ap()
    ig_o = nc.dram_tensor("ig_o", [4, TC], F32, kind="ExternalOutput").ap()
    fg_o = nc.dram_tensor("fg_o", [4, TC], F32, kind="ExternalOutput").ap()

    with contextlib.ExitStack() as es:
        pp = PB(nc, es)
        xml_sb = es.enter_context(nc.sbuf_tensor("xml_sb", [128, 16 * TH], BF16))
        xact_sb = es.enter_context(nc.sbuf_tensor("xact_sb", [128, 16 * TC], BF16))
        bq_bf = es.enter_context(nc.sbuf_tensor("bq_bf", [128, 16 * 128], BF16))
        bk_bf = es.enter_context(nc.sbuf_tensor("bk_bf", [128, 16 * 128], BF16))
        bv_bf = es.enter_context(nc.sbuf_tensor("bv_bf", [128, 16 * 128], BF16))
        stage = es.enter_context(nc.sbuf_tensor("stage", [128, 16 * 128], F32))
        stage2 = es.enter_context(nc.sbuf_tensor("stage2", [128, 16 * 128], F32))
        wiga_sb = es.enter_context(nc.sbuf_tensor("wiga_sb", [128, 16 * 8], F32))
        wigb_sb = es.enter_context(nc.sbuf_tensor("wigb_sb", [128, 16 * 8], F32))
        a_act = es.enter_context(nc.sbuf_tensor("a_act", [128, 16 * 8], BF16))
        a_ml = es.enter_context(nc.sbuf_tensor("a_ml", [128, 16 * 8], BF16))
        bg_sb = es.enter_context(nc.sbuf_tensor("bg_sb", [8, 1], F32))
        cw_sb = es.enter_context(nc.sbuf_tensor("cw_sb", [128, 64], F32))
        cb_sb = es.enter_context(nc.sbuf_tensor("cb_sb", [128, 16], F32))
        cv = es.enter_context(nc.sbuf_tensor("cv", [128, TC], F32))
        ev_bf = es.enter_context(nc.sbuf_tensor("ev_bf", [128, TT], BF16))
        g_f = es.enter_context(nc.sbuf_tensor("g_f", [8, TC], F32))
        pa = es.enter_context(nc.psum_tensor("pa", [128, TT], F32))
        pb = es.enter_context(nc.psum_tensor("pb", [128, TT], F32))
        pg = es.enter_context(nc.psum_tensor("pg", [8, TT], F32))
        p128 = es.enter_context(nc.psum_tensor("p128", [128, 128], F32))
        consts = make_consts(nc, es, pp, {"one": 1.0})

        def load_all(e, inc):
            for c in range(16):
                inc(e.dma_start(out=xml_sb[:, c * TH:(c + 1) * TH],
                                in_=xml[c * 128:(c + 1) * 128, :]), 16)
            inc(e.dma_start(out=cw_sb[:], in_=cw), 16)
            inc(e.dma_start(out=cb_sb[:], in_=cb), 16)
            inc(e.dma_start(out=bg_sb[:], in_=bg), 16)
        pp.phase(sync=load_all)

        for src, dst in ((bq, bq_bf), (bk, bk_bf), (bv, bv_bf)):
            def loadw(e, inc, src=src):
                for c in range(16):
                    inc(e.dma_start(out=stage[:, c * 128:(c + 1) * 128],
                                    in_=src[c * 128:(c + 1) * 128, :]), 16)
            pp.phase(sync=loadw)

            def castw(v, inc, dst=dst):
                ins = v.tensor_copy(out=dst[:], in_=stage[:])
                inc(ins)
            pp.phase(vector=castw)

        def load_fold1(e, inc):
            for c in range(16):
                inc(e.dma_start(out=stage[:, c * 128:(c + 1) * 128],
                                in_=bqT[c * 128:(c + 1) * 128, :]), 16)
                inc(e.dma_start(out=stage2[:, c * 128:(c + 1) * 128],
                                in_=bkT[c * 128:(c + 1) * 128, :]), 16)
                inc(e.dma_start(out=wiga_sb[:, c * 8:(c + 1) * 8],
                                in_=wig_q[c * 128:(c + 1) * 128, :]), 16)
                inc(e.dma_start(out=wigb_sb[:, c * 8:(c + 1) * 8],
                                in_=wig_k[c * 128:(c + 1) * 128, :]), 16)
        pp.phase(sync=load_fold1)

        def fold1(t, inc):
            for c in range(16):
                t.matmul(p128[:, c * 8:(c + 1) * 8], stage[:, c * 128:(c + 1) * 128],
                         wiga_sb[:, c * 8:(c + 1) * 8], start=True, stop=False)
                ins = t.matmul(p128[:, c * 8:(c + 1) * 8],
                               stage2[:, c * 128:(c + 1) * 128],
                               wigb_sb[:, c * 8:(c + 1) * 8], start=False, stop=True)
            inc(ins)
        pp.phase(tensor=fold1)

        def evf1(v, inc):
            ins = v.tensor_copy(out=a_act[:], in_=p128[:, 0:128])
            inc(ins)
        pp.phase(vector=evf1)

        def load_fold2(e, inc):
            for c in range(16):
                inc(e.dma_start(out=stage[:, c * 128:(c + 1) * 128],
                                in_=bvT[c * 128:(c + 1) * 128, :]), 16)
                inc(e.dma_start(out=wiga_sb[:, c * 8:(c + 1) * 8],
                                in_=wig_v[c * 128:(c + 1) * 128, :]), 16)
        pp.phase(sync=load_fold2)

        def fold2(t, inc):
            for c in range(16):
                ins = t.matmul(p128[:, c * 8:(c + 1) * 8],
                               stage[:, c * 128:(c + 1) * 128],
                               wiga_sb[:, c * 8:(c + 1) * 8], start=True, stop=True)
            inc(ins)
        pp.phase(tensor=fold2)

        def evf2(v, inc):
            ins = v.tensor_copy(out=a_ml[:], in_=p128[:, 0:128])
            inc(ins)
        pp.phase(vector=evf2)

        for c in range(16):
            def conv(v, inc, c=c):
                base = c * TH + 509
                v.tensor_scalar(out=cv[:], in0=xml_sb[:, base:base + TC],
                                scalar1=cw_sb[:, c * 4:c * 4 + 1], scalar2=None,
                                op0=ALU.mult)
                for i in (1, 2, 3):
                    ins = v.scalar_tensor_tensor(
                        out=cv[:], in0=xml_sb[:, base + i:base + i + TC],
                        scalar=cw_sb[:, c * 4 + i:c * 4 + i + 1], in1=cv[:],
                        op0=ALU.mult, op1=ALU.add)
                inc(ins)
            pp.phase(vector=conv)

            def act(s, inc, c=c):
                ins = s.activation(out=ch(xact_sb, c, TC), in_=cv[:], func=AF.Silu,
                                   bias=cb_sb[:, c:c + 1])
                inc(ins)
            pp.phase(scalar=act)

        def st_act(e, inc):
            for c in range(16):
                inc(e.dma_start(out=xact_o[c * 128:(c + 1) * 128, :],
                                in_=ch(xact_sb, c, TC)), 16)
        pp.phase(gpsimd=st_act)

        for w_bf, dram_out in ((bq_bf, q_o), (bk_bf, k_o)):
            for c in range(16):
                for tt in range(NT):
                    ps = pa if tt % 2 == 0 else pb

                    def mm(t, inc, c=c, tt=tt, ps=ps, w_bf=w_bf):
                        ins = t.matmul(ps[:, 0:TT], w_bf[:, c * 128:(c + 1) * 128],
                                       xact_sb[:, c * TC + tt * TT: c * TC + (tt + 1) * TT],
                                       start=True, stop=True)
                        inc(ins)
                    pp.phase(tensor=mm)

                    def ev(v, inc, ps=ps):
                        ins = v.tensor_copy(out=ev_bf[:], in_=ps[:, 0:TT])
                        inc(ins)
                    pp.phase(vector=ev)

                    def st(e, inc, c=c, tt=tt, dram_out=dram_out):
                        inc(e.dma_start(out=dram_out[c * 128:(c + 1) * 128,
                                                     tt * TT:(tt + 1) * TT],
                                        in_=ev_bf[:]), 16)
                    pp.phase(gpsimd=st)

        for tau in range(16):
            for g in range(4):
                def mm(t, inc, tau=tau, g=g):
                    for j in range(4):
                        c = 4 * g + j
                        ins = t.matmul(pa[:, j * 128:(j + 1) * 128],
                                       xml_sb[:, c * TH + 512 + tau * 128:
                                              c * TH + 512 + (tau + 1) * 128],
                                       bv_bf[:, c * 128:(c + 1) * 128],
                                       start=True, stop=True)
                    inc(ins)
                pp.phase(tensor=mm)

                def ev(v, inc):
                    ins = v.tensor_copy(out=ev_bf[:], in_=pa[:, 0:TT])
                    inc(ins)
                pp.phase(vector=ev)

                def st(e, inc, tau=tau, g=g):
                    inc(e.dma_start(out=v_o[tau * 128:(tau + 1) * 128,
                                           g * 512:(g + 1) * 512],
                                    in_=ev_bf[:]), 16)
                pp.phase(gpsimd=st)

        for tt in range(NT):
            def mm(t, inc, tt=tt):
                for c in range(16):
                    t.matmul(pg[:, 0:TT], a_act[:, c * 8:(c + 1) * 8],
                             xact_sb[:, c * TC + tt * TT: c * TC + (tt + 1) * TT],
                             start=(c == 0), stop=False)
                for c in range(16):
                    ins = t.matmul(pg[:, 0:TT], a_ml[:, c * 8:(c + 1) * 8],
                                   xml_sb[:, c * TH + 512 + tt * TT:
                                          c * TH + 512 + (tt + 1) * TT],
                                   start=False, stop=(c == 15))
                inc(ins)
            pp.phase(tensor=mm)

            def ev(v, inc, tt=tt):
                ins = v.tensor_scalar(out=g_f[:, tt * TT:(tt + 1) * TT],
                                      in0=pg[:, 0:TT], scalar1=bg_sb[:],
                                      scalar2=None, op0=ALU.add)
                inc(ins)
            pp.phase(vector=ev)

        def st_g(e, inc):
            inc(e.dma_start(out=ig_o, in_=g_f[0:4, :]), 16)
            inc(e.dma_start(out=fg_o, in_=g_f[4:8, :]), 16)
        pp.phase(gpsimd=st_g)

        pp.finish()
    return nc


QB = 256
NQB = TF // QB
DH = 512


def build_l5_attn():
    nc = bass.Bass("TRN2", target_bir_lowering=False)
    ins = {}
    for u in range(2):
        ins[f"qf{u}"] = nc.dram_tensor(f"qf{u}", [DH, TF], BF16, kind="ExternalInput").ap()
        ins[f"kf{u}"] = nc.dram_tensor(f"kf{u}", [DH, TF], BF16, kind="ExternalInput").ap()
        ins[f"v1{u}"] = nc.dram_tensor(f"v1{u}", [TF, DH], BF16, kind="ExternalInput").ap()
        ins[f"ig{u}"] = nc.dram_tensor(f"ig{u}", [1, TF], F32, kind="ExternalInput").ap()
        ins[f"fg{u}"] = nc.dram_tensor(f"fg{u}", [1, TF], F32, kind="ExternalInput").ap()
        ins[f"nw{u}"] = nc.dram_tensor(f"nw{u}", [128, DH], F32, kind="ExternalInput").ap()
    ident_in = nc.dram_tensor("ident_in", [128, 128], F32, kind="ExternalInput").ap()
    mbig_in = nc.dram_tensor("mbig_in", [128, 512], BF16, kind="ExternalInput").ap()
    outs = {}
    for u in range(2):
        outs[u] = nc.dram_tensor(f"hn{u}", [TF, DH], BF16, kind="ExternalOutput").ap()
    g3 = nc.dram_tensor("g3", [3, TF], F32, kind="Internal").ap()

    with contextlib.ExitStack() as es:
        pp = PB(nc, es)
        q_sb = es.enter_context(nc.sbuf_tensor("q_sb", [128, 4 * TF], BF16))
        k_sb = es.enter_context(nc.sbuf_tensor("k_sb", [128, 4 * TF], BF16))
        vbuf = es.enter_context(nc.sbuf_tensor("vbuf", [128, 32 * DH], BF16))
        ones_pv = es.enter_context(nc.sbuf_tensor("ones_pv", [128, 1], BF16))
        igr = es.enter_context(nc.sbuf_tensor("igr", [1, TF], F32))
        fgr = es.enter_context(nc.sbuf_tensor("fgr", [1, TF], F32))
        w1r = es.enter_context(nc.sbuf_tensor("w1r", [1, TF], F32))
        w2r = es.enter_context(nc.sbuf_tensor("w2r", [1, TF], F32))
        Fr = es.enter_context(nc.sbuf_tensor("Fr", [1, TF], F32))
        igmF = es.enter_context(nc.sbuf_tensor("igmF", [1, TF], F32))
        rmax = w2r
        nmd = igr
        t32 = es.enter_context(nc.sbuf_tensor("t32", [32, 128], F32))
        ident = es.enter_context(nc.sbuf_tensor("ident", [128, 128], F32))
        mbig = es.enter_context(nc.sbuf_tensor("mbig", [128, 512], BF16))
        nw_sb = es.enter_context(nc.sbuf_tensor("nw_sb", [128, DH], F32))
        igmF_tm = es.enter_context(nc.sbuf_tensor("igmF_tm", [128, 32], F32))
        rmax_tm = es.enter_context(nc.sbuf_tensor("rmax_tm", [128, 32], F32))
        encmD_tm = es.enter_context(nc.sbuf_tensor("encmD_tm", [128, 32], F32))
        cq_row = es.enter_context(nc.sbuf_tensor("cq_row", [1, NQB], F32))
        cq_b = es.enter_context(nc.sbuf_tensor("cq_b", [128, NQB], F32))
        ncq3 = es.enter_context(nc.sbuf_tensor("ncq3", [128, NQB], F32))
        colsc = es.enter_context(nc.sbuf_tensor("colsc", [128, 32], F32))
        rf = es.enter_context(nc.sbuf_tensor("rf", [128, 2], F32))
        T_sb = es.enter_context(nc.sbuf_tensor("T_sb", [128, QB], BF16))
        hsb = es.enter_context(nc.sbuf_tensor("hsb", [128, DH], F32))
        dsc = es.enter_context(nc.sbuf_tensor("dsc", [128, 1], F32))
        nrm = es.enter_context(nc.sbuf_tensor("nrm", [128, 1], F32))
        nrm2 = es.enter_context(nc.sbuf_tensor("nrm2", [128, 1], F32))
        mh_mu = es.enter_context(nc.sbuf_tensor("mh_mu", [128, 1], F32))
        mh_d = es.enter_context(nc.sbuf_tensor("mh_d", [128, DH], F32))
        mh_sq = es.enter_context(nc.sbuf_tensor("mh_sq", [128, DH], BF16))
        mh_vs = es.enter_context(nc.sbuf_tensor("mh_vs", [128, 1], F32))
        mh_r = es.enter_context(nc.sbuf_tensor("mh_r", [128, 1], F32))
        hn_bf = es.enter_context(nc.sbuf_tensor("hn_bf", [128, DH], BF16))
        ps_s = es.enter_context(nc.psum_tensor("ps_s", [128, QB], F32))
        ps_a0 = es.enter_context(nc.psum_tensor("ps_a0", [128, DH], F32))
        ps_a1 = es.enter_context(nc.psum_tensor("ps_a1", [128, DH], F32))
        ps_d0 = es.enter_context(nc.psum_tensor("ps_d0", [128, 1], F32))
        ps_d1 = es.enter_context(nc.psum_tensor("ps_d1", [128, 1], F32))
        ps_t = es.enter_context(nc.psum_tensor("ps_t", [128, 32], F32))
        ps_b = es.enter_context(nc.psum_tensor("ps_b", [128, NQB], F32))
        consts = make_consts(nc, es, pp, {"one": 1.0, "eps6": 1e-6, "m1e6": 1e-6})

        def load_const(e, inc):
            inc(e.dma_start(out=ident[:], in_=ident_in), 16)
            inc(e.dma_start(out=mbig[:], in_=mbig_in), 16)
        pp.phase(sync=load_const)

        def mk1(v, inc):
            ins = v.memset(ones_pv[:], 1.0)
            inc(ins)
        pp.phase(vector=mk1)

        for u in range(2):
            def load_u(e, inc, u=u):
                for c in range(4):
                    inc(e.dma_start(out=q_sb[:, c * TF:(c + 1) * TF],
                                    in_=ins[f"qf{u}"][c * 128:(c + 1) * 128, :]), 16)
                    inc(e.dma_start(out=k_sb[:, c * TF:(c + 1) * TF],
                                    in_=ins[f"kf{u}"][c * 128:(c + 1) * 128, :]), 16)
                for j in range(32):
                    inc(e.dma_start(out=vbuf[:, j * DH:(j + 1) * DH],
                                    in_=ins[f"v1{u}"][j * 128:(j + 1) * 128, :]), 16)
                inc(e.dma_start(out=igr[:], in_=ins[f"ig{u}"]), 16)
                inc(e.dma_start(out=fgr[:], in_=ins[f"fg{u}"]), 16)
                inc(e.dma_start(out=nw_sb[:], in_=ins[f"nw{u}"]), 16)
            pp.phase(sync=load_u)

            def lf1(s, inc):
                inc(s.activation(out=w1r[:], in_=fgr[:], func=AF.Abs), wait=True)
                inc(s.activation(out=w1r[:], in_=w1r[:], func=AF.Exp, scale=-1.0), wait=True)
                inc(s.activation(out=w1r[:], in_=w1r[:], func=AF.Ln,
                                 bias=consts["one"][0:1, :]), wait=True)
                inc(s.activation(out=w2r[:], in_=fgr[:], func=AF.Relu))
            pp.phase(scalar=lf1)

            def lf2(v, inc):
                inc(v.tensor_tensor(out=w2r[:], in0=fgr[:], in1=w2r[:], op=ALU.subtract), wait=True)
                inc(v.tensor_tensor(out=w2r[:], in0=w2r[:], in1=w1r[:], op=ALU.subtract), wait=True)
                inc(v.memset(w1r[:], 1.0), wait=True)
                inc(v.tensor_tensor_scan(out=Fr[:], data0=w1r[:], data1=w2r[:],
                                         initial=0.0, op0=ALU.mult, op1=ALU.add), wait=True)
                inc(v.tensor_tensor(out=igmF[:], in0=igr[:], in1=Fr[:], op=ALU.subtract), wait=True)
                inc(v.tensor_tensor_scan(out=rmax[:], data0=igmF[:], data1=igmF[:],
                                         initial=NEG_BIG, op0=ALU.max, op1=ALU.max), wait=True)
                inc(v.tensor_tensor(out=nmd[:], in0=Fr[:], in1=rmax[:], op=ALU.add), wait=True)
                inc(v.tensor_scalar(out=nmd[:], in0=nmd[:], scalar1=-1.0,
                                    scalar2=None, op0=ALU.mult))
            pp.phase(vector=lf2)

            def to_dram(e, inc):
                inc(e.dma_start(out=g3[0:1, :], in_=igmF[:]), 16)
                inc(e.dma_start(out=g3[1:2, :], in_=rmax[:]), 16)
                inc(e.dma_start(out=g3[2:3, :], in_=nmd[:]), 16)
            pp.phase(gpsimd=to_dram)

            for r, dst in ((0, igmF_tm), (1, rmax_tm), (2, encmD_tm)):
                def ld32(e, inc, r=r):
                    inc(e.dma_start(out=t32[:],
                                    in_=g3[r:r + 1, :].rearrange("one (a b) -> (one a) b", b=128)), 16)
                pp.phase(sync=ld32)

                def tp(t, inc):
                    ins = t.transpose(out=ps_t[:], in_=t32[:], identity=ident[0:32, 0:32])
                    inc(ins)
                pp.phase(tensor=tp)

                if r < 2:
                    def cp(v, inc, dst=dst):
                        ins = v.tensor_copy(out=dst[:], in_=ps_t[:])
                        inc(ins)
                    pp.phase(vector=cp)
                else:
                    def cp(s, inc, dst=dst):
                        ins = s.activation(out=dst[:], in_=ps_t[:], func=AF.Exp)
                        inc(ins)
                    pp.phase(scalar=cp)

            def cq1(v, inc):
                ins = v.tensor_copy(out=cq_row[:], in_=rmax[0:1, QB - 1:TF:QB])
                inc(ins)
            pp.phase(vector=cq1)

            def cq2(t, inc):
                ins = t.matmul(ps_b[:, 0:NQB], consts["ones_row"], cq_row[:],
                               start=True, stop=True)
                inc(ins)
            pp.phase(tensor=cq2)

            def cq3(v, inc):
                v.tensor_copy(out=cq_b[:], in_=ps_b[:, 0:NQB])
                ins = v.tensor_scalar(out=ncq3[:], in0=ps_b[:, 0:NQB],
                                      scalar1=CSC_SHIFT, scalar2=-1.0,
                                      op0=ALU.add, op1=ALU.mult)
                inc(ins)
            pp.phase(vector=cq3)

            for qb in range(NQB):
                nk = 2 * qb + 2

                def gsc(s, inc, qb=qb, nk=nk):
                    s.activation(out=colsc[:, 0:nk], in_=igmF_tm[:, 0:nk], func=AF.Exp,
                                 bias=ncq3[:, qb:qb + 1])
                    ins = s.activation(out=rf[:], in_=rmax_tm[:, 2 * qb:2 * qb + 2],
                                       func=AF.Exp, scale=-1.0, bias=cq_b[:, qb:qb + 1])
                    inc(ins)
                pp.phase(scalar=gsc)

                for j in range(nk):
                    def mm_s(t, inc, qb=qb, j=j):
                        for c in range(4):
                            ins = t.matmul(ps_s[:, 0:QB],
                                           k_sb[:, c * TF + j * 128: c * TF + (j + 1) * 128],
                                           q_sb[:, c * TF + qb * QB: c * TF + (qb + 1) * QB],
                                           start=(c == 0), stop=(c == 3))
                        inc(ins)
                    pp.phase(tensor=mm_s)

                    diag_r = j - 2 * qb

                    def evs(v, inc, j=j, diag_r=diag_r):
                        if diag_r in (0, 1):
                            ins = v.scalar_tensor_tensor(
                                out=T_sb[:], in0=ps_s[:, 0:QB],
                                scalar=colsc[:, j:j + 1],
                                in1=mbig[:, 256 - 128 * diag_r: 512 - 128 * diag_r],
                                op0=ALU.mult, op1=ALU.mult)
                        else:
                            ins = v.tensor_scalar(out=T_sb[:], in0=ps_s[:, 0:QB],
                                                  scalar1=colsc[:, j:j + 1],
                                                  scalar2=None, op0=ALU.mult)
                        inc(ins)
                    pp.phase(vector=evs)

                    def mm_v(t, inc, j=j, nk=nk):
                        st_, sp_ = (j == 0), (j == nk - 1)
                        t.matmul(ps_a0[:, 0:DH], T_sb[:, 0:128],
                                 vbuf[:, j * DH:(j + 1) * DH], start=st_, stop=sp_)
                        t.matmul(ps_a1[:, 0:DH], T_sb[:, 128:256],
                                 vbuf[:, j * DH:(j + 1) * DH], start=st_, stop=sp_)
                        t.matmul(ps_d0[:, 0:1], T_sb[:, 0:128], ones_pv[:],
                                 start=st_, stop=sp_)
                        ins = t.matmul(ps_d1[:, 0:1], T_sb[:, 128:256], ones_pv[:],
                                       start=st_, stop=sp_)
                        inc(ins)
                    pp.phase(tensor=mm_v)

                for qc in range(2):
                    ps_acc = ps_a0 if qc == 0 else ps_a1
                    ps_den = ps_d0 if qc == 0 else ps_d1
                    tmc = 2 * qb + qc

                    def ep1(v, inc, ps_acc=ps_acc, ps_den=ps_den, qc=qc):
                        v.tensor_scalar(out=hsb[:], in0=ps_acc[:, 0:DH],
                                        scalar1=rf[:, qc:qc + 1], scalar2=None,
                                        op0=ALU.mult)
                        ins = v.tensor_scalar(out=dsc[:], in0=ps_den[:, 0:1],
                                              scalar1=rf[:, qc:qc + 1], scalar2=None,
                                              op0=ALU.mult)
                        inc(ins)
                    pp.phase(vector=ep1)

                    def ep1b(s, inc):
                        ins = s.activation(out=nrm[:], in_=dsc[:], func=AF.Abs)
                        inc(ins)
                    pp.phase(scalar=ep1b)

                    def ep2(v, inc, tmc=tmc):
                        ins = v.tensor_tensor(out=nrm[:], in0=nrm[:],
                                              in1=encmD_tm[:, tmc:tmc + 1], op=ALU.max)
                        inc(ins)
                    pp.phase(vector=ep2)

                    def ep3(s, inc):
                        inc(s.activation(out=nrm2[:], in_=nrm[:], func=AF.Ln,
                                         bias=consts["m1e6"]), wait=True)
                        inc(s.activation(out=nrm2[:], in_=nrm2[:], func=AF.Exp,
                                         scale=-1.0))
                    pp.phase(scalar=ep3)

                    def ep4a(v, inc):
                        v.tensor_scalar(out=mh_d[:], in0=hsb[:],
                                        scalar1=nrm2[:], scalar2=None, op0=ALU.mult)
                        ins = v.tensor_reduce(out=mh_mu[:], in_=mh_d[:], axis=AX,
                                              op=ALU.add)
                        inc(ins)
                    pp.phase(vector=ep4a)

                    def ep4b(s, inc):
                        ins = s.mul(out=mh_mu[:], in_=mh_mu[:], mul=1.0 / DH)
                        inc(ins)
                    pp.phase(scalar=ep4b)

                    def ep4c(v, inc):
                        ins = v.tensor_scalar(out=mh_d[:], in0=mh_d[:],
                                              scalar1=mh_mu[:], scalar2=None,
                                              op0=ALU.subtract)
                        inc(ins)
                    pp.phase(vector=ep4c)

                    def ep5(s, inc):
                        inc(s.activation(out=mh_sq[:], in_=mh_d[:], func=AF.Square,
                                         accum_out=mh_vs[:]), wait=True)
                        inc(s.activation(out=mh_r[:], in_=mh_vs[:], func=AF.Ln,
                                         scale=1.0 / DH, bias=consts["eps6"]), wait=True)
                        inc(s.activation(out=mh_r[:], in_=mh_r[:], func=AF.Exp,
                                         scale=-0.5))
                    pp.phase(scalar=ep5)

                    def ep6(v, inc):
                        v.tensor_scalar(out=mh_d[:], in0=mh_d[:], scalar1=mh_r[:],
                                        scalar2=None, op0=ALU.mult)
                        ins = v.tensor_tensor(out=hn_bf[:], in0=mh_d[:], in1=nw_sb[:],
                                              op=ALU.mult)
                        inc(ins)
                    pp.phase(vector=ep6)

                    def ep7(e, inc, qb=qb, qc=qc, u=u):
                        r0 = qb * QB + qc * 128
                        inc(e.dma_start(out=outs[u][r0:r0 + 128, :], in_=hn_bf[:]), 16)
                    pp.phase(gpsimd=ep7)

        pp.finish()
    return nc



def build_l6a_post():
    nc = bass.Bass("TRN2", target_bir_lowering=False)
    x2 = nc.dram_tensor("x2", [C, TC], F32, kind="ExternalInput").ap()
    xact = nc.dram_tensor("xact", [2048, TC], BF16, kind="ExternalInput").ap()
    z = nc.dram_tensor("z", [2048, TC], BF16, kind="ExternalInput").ap()
    hn = nc.dram_tensor("hn", [2048, TC], BF16, kind="ExternalInput").ap()
    skip = nc.dram_tensor("skip", [128, 16], F32, kind="ExternalInput").ap()
    wdn = nc.dram_tensor("wdn", [2048, C], F32, kind="ExternalInput").ap()
    x3_o = nc.dram_tensor("x3_o", [C, TC], F32, kind="ExternalOutput").ap()
    with contextlib.ExitStack() as es:
        pp = PB(nc, es)
        stage = es.enter_context(nc.sbuf_tensor("stage", [128, 4 * C], F32))
        w_bf = es.enter_context(nc.sbuf_tensor("w_bf", [128, 16 * C], BF16))
        xa_t = es.enter_context(nc.sbuf_tensor("xa_t", [128, 16 * TT], BF16))
        z_t = es.enter_context(nc.sbuf_tensor("z_t", [128, 16 * TT], BF16))
        hn_t = es.enter_context(nc.sbuf_tensor("hn_t", [128, 16 * TT], BF16))
        x2_t = es.enter_context(nc.sbuf_tensor("x2_t", [128, 8 * TT], F32))
        ao_f = es.enter_context(nc.sbuf_tensor("ao_f", [128, 8 * TT], F32))
        sz = es.enter_context(nc.sbuf_tensor("sz", [128, TT], F32))
        tmf = es.enter_context(nc.sbuf_tensor("tmf", [128, TT], F32))
        skip_sb = es.enter_context(nc.sbuf_tensor("skip_sb", [128, 16], F32))
        kit = LNKit(nc, es, "ln", 8, TT)
        pa = es.enter_context(nc.psum_tensor("pa", [128, TT], F32))
        pb = es.enter_context(nc.psum_tensor("pb", [128, TT], F32))
        consts = make_consts(nc, es, pp, {"eps5": 1e-5, "one": 1.0})

        for rnd in range(4):
            def loadw(e, inc, rnd=rnd):
                for j in range(4):
                    kc = rnd * 4 + j
                    inc(e.dma_start(out=stage[:, j * C:(j + 1) * C],
                                    in_=wdn[kc * 128:(kc + 1) * 128, :]), 16)
            pp.phase(sync=loadw)

            def castw(v, inc, rnd=rnd):
                ins = v.tensor_copy(out=w_bf[:, rnd * 4 * C:(rnd + 1) * 4 * C],
                                    in_=stage[:])
                inc(ins)
            pp.phase(vector=castw)

        def loads(e, inc):
            inc(e.dma_start(out=skip_sb[:], in_=skip), 16)
        pp.phase(sync=loads)

        for tt in range(NT):
            t0 = tt * TT

            def load_t(e, inc, t0=t0):
                load_fm(e, inc, xa_t, xact, 16, TT, t0)
                load_fm(e, inc, z_t, z, 16, TT, t0)
                load_fm(e, inc, hn_t, hn, 16, TT, t0)
                load_fm(e, inc, x2_t, x2, 8, TT, t0)
            pp.phase(sync=load_t)

            for c in range(16):
                def hs1(s, inc, c=c):
                    ins = s.activation(out=sz[:], in_=ch(z_t, c, TT), func=AF.Silu)
                    inc(ins)
                pp.phase(scalar=hs1)

                def hs2(v, inc, c=c):
                    v.scalar_tensor_tensor(out=tmf[:], in0=ch(xa_t, c, TT),
                                           scalar=skip_sb[:, c:c + 1],
                                           in1=ch(hn_t, c, TT),
                                           op0=ALU.mult, op1=ALU.add)
                    ins = v.tensor_tensor(out=ch(hn_t, c, TT), in0=tmf[:], in1=sz[:],
                                          op=ALU.mult)
                    inc(ins)
                pp.phase(vector=hs2)

            for dc in range(8):
                ps = pa if dc % 2 == 0 else pb

                def mm(t, inc, dc=dc, ps=ps):
                    for kc in range(16):
                        ins = t.matmul(ps[:, 0:TT],
                                       w_bf[:, kc * C + dc * 128: kc * C + (dc + 1) * 128],
                                       ch(hn_t, kc, TT), start=(kc == 0), stop=(kc == 15))
                    inc(ins)
                pp.phase(tensor=mm)

                def ev(v, inc, dc=dc, ps=ps):
                    ins = v.tensor_copy(out=ch(ao_f, dc, TT), in_=ps[:, 0:TT])
                    inc(ins)
                pp.phase(vector=ev)

            emit_ln(pp, kit, consts, ao_f, pa, pb, out_f32=ao_f)

            def addres(v, inc):
                for c in range(8):
                    ins = v.tensor_tensor(out=ch(x2_t, c, TT), in0=ch(x2_t, c, TT),
                                          in1=ch(ao_f, c, TT), op=ALU.add)
                inc(ins)
            pp.phase(vector=addres)

            def store(e, inc, t0=t0):
                store_fm(e, inc, x3_o, x2_t, 8, TT, t0)
            pp.phase(gpsimd=store)
        pp.finish()
    return nc


def build_l6b_ffn1():
    nc = bass.Bass("TRN2", target_bir_lowering=False)
    x3 = nc.dram_tensor("x3", [C, TC], F32, kind="ExternalInput").ap()
    w1 = nc.dram_tensor("w1", [C, 4096], F32, kind="ExternalInput").ap()
    b1 = nc.dram_tensor("b1", [128, 32], F32, kind="ExternalInput").ap()
    w2 = nc.dram_tensor("w2", [4096, C], F32, kind="ExternalInput").ap()
    b2 = nc.dram_tensor("b2", [128, 8], F32, kind="ExternalInput").ap()
    x4_o = nc.dram_tensor("x4_o", [C, TC], F32, kind="ExternalOutput").ap()
    w2bf_d = nc.dram_tensor("w2bf_d", [4096, C], BF16, kind="Internal").ap()
    with contextlib.ExitStack() as es:
        pp = PB(nc, es)
        stage = es.enter_context(nc.sbuf_tensor("stage", [128, 4096], F32))
        w1_bf = es.enter_context(nc.sbuf_tensor("w1_bf", [128, 8 * 4096], BF16))
        w2t = es.enter_context(nc.sbuf_tensor("w2t", [128, C], BF16))
        b1_sb = es.enter_context(nc.sbuf_tensor("b1_sb", [128, 32], F32))
        b2_sb = es.enter_context(nc.sbuf_tensor("b2_sb", [128, 8], F32))
        x3_t = es.enter_context(nc.sbuf_tensor("x3_t", [128, 8 * TT], F32))
        fin_bf = es.enter_context(nc.sbuf_tensor("fin_bf", [128, 8 * TT], BF16))
        hh_bf = es.enter_context(nc.sbuf_tensor("hh_bf", [128, 32 * TT], BF16))
        ffn_f = es.enter_context(nc.sbuf_tensor("ffn_f", [128, 8 * TT], F32))
        kit = LNKit(nc, es, "ln", 8, TT)
        pa = es.enter_context(nc.psum_tensor("pa", [128, TT], F32))
        pb = es.enter_context(nc.psum_tensor("pb", [128, TT], F32))
        pc0 = es.enter_context(nc.psum_tensor("pc0", [128, TT], F32))
        pc1 = es.enter_context(nc.psum_tensor("pc1", [128, TT], F32))
        pc2 = es.enter_context(nc.psum_tensor("pc2", [128, TT], F32))
        pc3 = es.enter_context(nc.psum_tensor("pc3", [128, TT], F32))
        pcs = [pc0, pc1, pc2, pc3]
        consts = make_consts(nc, es, pp, {"eps5": 1e-5, "one": 1.0})

        for kc in range(8):
            def loadw1(e, inc, kc=kc):
                inc(e.dma_start(out=stage[:], in_=w1[kc * 128:(kc + 1) * 128, :]), 16)
            pp.phase(sync=loadw1)

            def castw1(v, inc, kc=kc):
                ins = v.tensor_copy(out=w1_bf[:, kc * 4096:(kc + 1) * 4096], in_=stage[:])
                inc(ins)
            pp.phase(vector=castw1)

        def load_b(e, inc):
            inc(e.dma_start(out=b1_sb[:], in_=b1), 16)
            inc(e.dma_start(out=b2_sb[:], in_=b2), 16)
        pp.phase(sync=load_b)

        for kc in range(32):
            def loadw2(e, inc, kc=kc):
                inc(e.dma_start(out=stage[:, 0:C], in_=w2[kc * 128:(kc + 1) * 128, :]), 16)
            pp.phase(sync=loadw2)

            def castw2(v, inc):
                ins = v.tensor_copy(out=w2t[:], in_=stage[:, 0:C])
                inc(ins)
            pp.phase(vector=castw2)

            def stw2(e, inc, kc=kc):
                inc(e.dma_start(out=w2bf_d[kc * 128:(kc + 1) * 128, :], in_=w2t[:]), 16)
            pp.phase(gpsimd=stw2)

        for tt in range(NT):
            t0 = tt * TT

            def load_x(e, inc, t0=t0):
                load_fm(e, inc, x3_t, x3, 8, TT, t0)
            pp.phase(sync=load_x)

            emit_ln(pp, kit, consts, x3_t, pa, pb, out_bf=fin_bf)

            for hcol in range(32):
                ps = pa if hcol % 2 == 0 else pb

                def mm1(t, inc, hcol=hcol, ps=ps):
                    for kc in range(8):
                        ins = t.matmul(
                            ps[:, 0:TT],
                            w1_bf[:, kc * 4096 + hcol * 128: kc * 4096 + (hcol + 1) * 128],
                            ch(fin_bf, kc, TT), start=(kc == 0), stop=(kc == 7))
                    inc(ins)
                pp.phase(tensor=mm1)

                def gl(s, inc, hcol=hcol, ps=ps):
                    ins = s.activation(out=ch(hh_bf, hcol, TT), in_=ps[:, 0:TT],
                                       func=AF.Gelu, bias=b1_sb[:, hcol:hcol + 1])
                    inc(ins)
                pp.phase(scalar=gl)

            for dcg in range(2):
                for kc in range(32):
                    def loadw2t(e, inc, kc=kc):
                        inc(e.dma_start(out=w2t[:],
                                        in_=w2bf_d[kc * 128:(kc + 1) * 128, :]), 16)
                    pp.phase(sync=loadw2t)

                    def mm2(t, inc, kc=kc, dcg=dcg):
                        for j in range(4):
                            dc = dcg * 4 + j
                            ins = t.matmul(pcs[j][:, 0:TT],
                                           w2t[:, dc * 128:(dc + 1) * 128],
                                           ch(hh_bf, kc, TT),
                                           start=(kc == 0), stop=(kc == 31))
                        inc(ins)
                    pp.phase(tensor=mm2)

                def ev2(v, inc, dcg=dcg):
                    for j in range(4):
                        dc = dcg * 4 + j
                        ins = v.tensor_scalar(out=ch(ffn_f, dc, TT),
                                              in0=pcs[j][:, 0:TT],
                                              scalar1=b2_sb[:, dc:dc + 1],
                                              scalar2=None, op0=ALU.add)
                    inc(ins)
                pp.phase(vector=ev2)

            emit_ln(pp, kit, consts, ffn_f, pa, pb, out_f32=ffn_f)

            def addres(v, inc):
                for c in range(8):
                    ins = v.tensor_tensor(out=ch(x3_t, c, TT), in0=ch(x3_t, c, TT),
                                          in1=ch(ffn_f, c, TT), op=ALU.add)
                inc(ins)
            pp.phase(vector=addres)

            def store(e, inc, t0=t0):
                store_fm(e, inc, x4_o, x3_t, 8, TT, t0)
            pp.phase(gpsimd=store)
        pp.finish()
    return nc



def _asc(a):
    return np.ascontiguousarray(a)


def _blockdiag(W):
    blocks = W.reshape(16, 32, 4, 4)
    Bd = np.zeros((16, 128, 128), np.float32)
    BdT = np.zeros((16, 128, 128), np.float32)
    for a in range(32):
        Bd[:, 4 * a:4 * a + 4, 4 * a:4 * a + 4] = blocks[:, a].transpose(0, 2, 1)
        BdT[:, 4 * a:4 * a + 4, 4 * a:4 * a + 4] = blocks[:, a]
    return Bd.reshape(16 * 128, 128), BdT.reshape(16 * 128, 128)


def _cols(vec, nch):
    return _asc(np.asarray(vec, np.float32).reshape(nch, 128).T)


_CACHE = {}


def _get(name, builder):
    if name not in _CACHE:
        _CACHE[name] = builder()
    return _CACHE[name]


def kernel(**inputs):
    f32 = np.float32
    x0 = np.asarray(inputs["L0_attn_x"], f32)
    cores = [(b, h) for b in range(B) for h in range(2)]

    wiT = _asc(np.asarray(inputs["L0_attn_Wi"], f32).T)
    wrT = _asc(np.asarray(inputs["L0_attn_Wr"], f32).T)
    bi = _cols(inputs["L0_attn_bi"], 8)
    br = _cols(inputs["L0_attn_br"], 8)
    lam = np.asarray(inputs["L0_attn_cap_lambda"], f32).reshape(1, 1)
    im1 = []
    for (b, h) in cores:
        xs = _asc(x0[b, h * TC:(h + 1) * TC, :].T)
        im1.append({"xs": xs, "wiT": wiT, "wrT": wrT, "bi": bi, "br": br, "lam": lam})
    r1 = run_bass_kernel_spmd(_get("l1", build_l1_gates), im1,
                              core_ids=list(range(NCORES))).results

    at_fm = [np.concatenate([r1[2 * b]["at_o"], r1[2 * b + 1]["at_o"]], axis=1)
             for b in range(B)]
    gx_fm = [np.concatenate([r1[2 * b]["gx_o"], r1[2 * b + 1]["gx_o"]], axis=1)
             for b in range(B)]

    im2 = []
    for b in range(B):
        for f in range(2):
            im2.append({"at_i": _asc(at_fm[b][f * 512:(f + 1) * 512]),
                        "gx_i": _asc(gx_fm[b][f * 512:(f + 1) * 512])})
    r2 = run_bass_kernel_spmd(_get("l2", build_l2_scan), im2,
                              core_ids=list(range(NCORES))).results
    ys_fm = [np.concatenate([r2[2 * b]["ys_o"], r2[2 * b + 1]["ys_o"]], axis=0)
             for b in range(B)]

    th = _asc(np.asarray(inputs["L0_ffn_theta"], f32))
    thn = _asc(np.asarray(inputs["L0_ffn_theta_noise"], f32))
    phm = _asc(np.asarray(inputs["L0_ffn_phi"], f32))
    phn = _asc(np.asarray(inputs["L0_ffn_phi_noise"], f32))
    im3 = []
    for (b, h) in cores:
        im3.append({"xs": _asc(x0[b, h * TC:(h + 1) * TC, :].T),
                    "ys": _asc(ys_fm[b][:, h * TC:(h + 1) * TC]),
                    "th": th, "thn": thn, "ph": phm, "phn": phn})
    r3 = run_bass_kernel_spmd(_get("l3", build_l3_ffn0), im3,
                              core_ids=list(range(NCORES))).results
    x2 = [r3[i]["x2_o"] for i in range(NCORES)]

    wup = _asc(np.asarray(inputs["L1_attn_W_up"], f32))
    im4a = []
    for i, (b, h) in enumerate(cores):
        x2h = np.zeros((C, TH), f32)
        x2h[:, 512:] = x2[i]
        if h == 1:
            x2h[:, 509:512] = x2[2 * b][:, TC - 3:]
        im4a.append({"x2h": x2h, "wup": wup})
    r4a = run_bass_kernel_spmd(_get("l4a", build_l4a_up), im4a,
                               core_ids=list(range(NCORES))).results

    bq, bqT = _blockdiag(np.asarray(inputs["L1_attn_Wq"], f32))
    bk, bkT = _blockdiag(np.asarray(inputs["L1_attn_Wk"], f32))
    bv, bvT = _blockdiag(np.asarray(inputs["L1_attn_Wv"], f32))
    W_ig = np.asarray(inputs["L1_attn_W_ig"], f32)
    W_fg = np.asarray(inputs["L1_attn_W_fg"], f32)
    wig_q = _asc(np.concatenate([W_ig[0:2048], W_fg[0:2048]], axis=1))
    wig_k = _asc(np.concatenate([W_ig[2048:4096], W_fg[2048:4096]], axis=1))
    wig_v = _asc(np.concatenate([W_ig[4096:6144], W_fg[4096:6144]], axis=1))
    bg = _asc(np.concatenate([np.asarray(inputs["L1_attn_b_ig"], f32),
                              np.asarray(inputs["L1_attn_b_fg"], f32)]).reshape(8, 1))
    cw = _asc(np.asarray(inputs["L1_attn_conv_w"], f32)
              .reshape(4, 16, 128).transpose(2, 1, 0).reshape(128, 64))
    cb = _cols(inputs["L1_attn_conv_b"], 16)
    im4b = []
    for i in range(NCORES):
        im4b.append({"xml": r4a[i]["xml_o"], "bq": bq, "bk": bk, "bv": bv,
                     "bqT": bqT, "bkT": bkT, "bvT": bvT,
                     "wig_q": wig_q, "wig_k": wig_k, "wig_v": wig_v,
                     "bg": bg, "cw": cw, "cb": cb})
    r4b = run_bass_kernel_spmd(_get("l4b", build_l4b_qkv), im4b,
                               core_ids=list(range(NCORES))).results

    norm_w = np.asarray(inputs["L1_attn_norm_w"], f32)
    ident = np.eye(128, dtype=f32)
    mbig = (np.arange(512)[None, :] >= (np.arange(128)[:, None] + 256)).astype(BF)
    im5 = []
    for i in range(NCORES):
        b, npair = i // 2, i % 2
        m = {"ident_in": ident, "mbig_in": mbig}
        for u in range(2):
            n = 2 * npair + u
            c0, c1 = 2 * b, 2 * b + 1
            m[f"qf{u}"] = _asc(np.concatenate(
                [r4b[c0]["q_o"][512 * n:512 * (n + 1)],
                 r4b[c1]["q_o"][512 * n:512 * (n + 1)]], axis=1))
            m[f"kf{u}"] = _asc(np.concatenate(
                [r4b[c0]["k_o"][512 * n:512 * (n + 1)],
                 r4b[c1]["k_o"][512 * n:512 * (n + 1)]], axis=1))
            vfull = np.concatenate([r4b[c0]["v_o"], r4b[c1]["v_o"]], axis=0)
            m[f"v1{u}"] = _asc(vfull[:, 512 * n:512 * (n + 1)])
            m[f"ig{u}"] = _asc(np.concatenate(
                [r4b[c0]["ig_o"][n:n + 1], r4b[c1]["ig_o"][n:n + 1]], axis=1))
            m[f"fg{u}"] = _asc(np.concatenate(
                [r4b[c0]["fg_o"][n:n + 1], r4b[c1]["fg_o"][n:n + 1]], axis=1))
            m[f"nw{u}"] = _asc(np.tile(norm_w[None, 512 * n:512 * (n + 1)], (128, 1)))
        im5.append(m)
    r5 = run_bass_kernel_spmd(_get("l5", build_l5_attn), im5,
                              core_ids=list(range(NCORES))).results

    skip_sb = _cols(inputs["L1_attn_skip"], 16)
    wdn = _asc(np.asarray(inputs["L1_attn_W_down"], f32))
    im6a = []
    for i, (b, h) in enumerate(cores):
        hn_parts = []
        for n in range(4):
            core5 = 2 * b + n // 2
            u = n % 2
            hn_parts.append(r5[core5][f"hn{u}"][h * TC:(h + 1) * TC, :])
        hn_tm = np.concatenate(hn_parts, axis=1)
        im6a.append({"x2": x2[i], "xact": r4b[i]["xact_o"], "z": r4a[i]["z_o"],
                     "hn": _asc(hn_tm.T), "skip": skip_sb, "wdn": wdn})
    r6a = run_bass_kernel_spmd(_get("l6a", build_l6a_post), im6a,
                               core_ids=list(range(NCORES))).results

    w1 = _asc(np.asarray(inputs["L1_ffn_w1"], f32))
    b1c = _cols(inputs["L1_ffn_b1"], 32)
    w2 = _asc(np.asarray(inputs["L1_ffn_w2"], f32))
    b2c = _cols(inputs["L1_ffn_b2"], 8)
    im6b = [{"x3": r6a[i]["x3_o"], "w1": w1, "b1": b1c, "w2": w2, "b2": b2c}
            for i in range(NCORES)]
    r6b = run_bass_kernel_spmd(_get("l6b", build_l6b_ffn1), im6b,
                               core_ids=list(range(NCORES))).results

    out = np.empty((B, TF, C), f32)
    for i, (b, h) in enumerate(cores):
        out[b, h * TC:(h + 1) * TC, :] = r6b[i]["x4_o"].T
    return out
